# version 03
# baseline (speedup 1.0000x reference)
import math
from contextlib import ExitStack

import numpy as np
import concourse.bass as bass
import concourse.mybir as mybir
from concourse.bass_utils import run_bass_kernel_spmd

F32 = mybir.dt.float32
BF16 = mybir.dt.bfloat16
AF = mybir.ActivationFunctionType
ALU = mybir.AluOpType
AX = mybir.AxisListType

NEG_BIG = -30000.0


def real_cfg():
    return dict(
        d=2048,
        hv=16, hk=8,
        dk=128, dv=128,
        tb=4096,
        tf=2048,
        i_sh=5632, i_e=1408, n_e=8,
        n_cores=8,
        pair_groups=[[0, 1], [2, 3], [4, 5], [6, 7]],
    )



class Buf:
    __slots__ = ("name", "last_w", "readers")

    def __init__(self, name):
        self.name = name
        self.last_w = None
        self.readers = []


class Prog:
    COMPUTE_SEM = {"tensor": "pe", "vector": "dve", "scalar": "act",
                   "gpsimd": "pool", "sync": "sp"}
    DMA_QUEUES = {"sync": "sdma", "gpsimd": "gdma", "scalar": "adma"}
    NLANES = 8
    ENGINES = ["tensor", "vector", "scalar", "gpsimd", "sync"]

    def __init__(self, nc):
        self.nc = nc
        self.cnt = {}
        self.sems = {}
        self.sat = {e: {} for e in self.ENGINES}
        self.steps = {e: [] for e in self.ENGINES}
        self._stack = ExitStack()
        keys = ["pe", "dve", "act", "pool", "sp", "cc"]
        for q in self.DMA_QUEUES.values():
            keys += [f"{q}{i}" for i in range(self.NLANES)]
        for key in keys:
            self.sems[key] = self._stack.enter_context(nc.semaphore(key))
            self.cnt[key] = 0
        self.qrr = {q: 0 for q in self.DMA_QUEUES.values()}
        self.nsteps = 0

    def buf(self, name):
        return Buf(name)

    def step(self, engine, fn, reads=(), writes=(), semkey=None, inc=None,
             extra_waits=()):
        if semkey is None:
            semkey = self.COMPUTE_SEM[engine]
            if inc is None:
                inc = 1
        waits = {}
        for k, c in extra_waits:
            if waits.get(k, 0) < c:
                waits[k] = c

        def need(dep):
            k, c = dep
            if waits.get(k, 0) < c:
                waits[k] = c

        for b in reads:
            if b.last_w is not None:
                need(b.last_w)
        for b in writes:
            if b.last_w is not None:
                need(b.last_w)
            for r in b.readers:
                need(r)
        sat = self.sat[engine]
        final_waits = []
        for k, c in sorted(waits.items()):
            if sat.get(k, 0) >= c:
                continue
            final_waits.append((self.sems[k], c))
            sat[k] = c
        rec = None
        if inc:
            self.cnt[semkey] += inc
            rec = (semkey, self.cnt[semkey])
        for b in writes:
            assert rec is not None, "writing step must inc a semaphore"
            b.last_w = rec
            b.readers = []
        if rec is not None:
            for b in reads:
                b.readers.append(rec)
        self.steps[engine].append(
            (final_waits, fn, (self.sems[semkey], inc) if inc else None))
        self.nsteps += 1

    def dma(self, queue_engine, fn, reads=(), writes=()):
        q = self.DMA_QUEUES[queue_engine]
        lane = self.qrr[q] % self.NLANES
        self.qrr[q] += 1
        semkey = f"{q}{lane}"
        prev = self.cnt[semkey]
        ew = [(semkey, prev)] if prev else []
        self.step(queue_engine, fn, reads=reads, writes=writes,
                  semkey=semkey, inc=16, extra_waits=ew)

    def barrier(self):
        snapshot = dict(self.cnt)
        for e in self.ENGINES:
            sat = self.sat[e]
            fw = []
            for k, c in snapshot.items():
                if c and sat.get(k, 0) < c:
                    fw.append((self.sems[k], c))
                    sat[k] = c
            if fw:
                self.steps[e].append((fw, None, None))

    def finalize(self):
        nc = self.nc

        def mk(engine_name):
            def run(eng):
                for waits, fn, inc in self.steps[engine_name]:
                    for sem, val in waits:
                        eng.wait_ge(sem, val)
                    if fn is not None:
                        ins = fn(eng)
                        if inc is not None:
                            assert ins is not None, f"step on {engine_name} returned no instruction"
                            ins.then_inc(inc[0], inc[1])
                    else:
                        assert inc is None
            return run

        with nc.Block() as block:
            block.tensor(mk("tensor"))
            block.vector(mk("vector"))
            block.scalar(mk("scalar"))
            block.gpsimd(mk("gpsimd"))
            block.sync(mk("sync"))

    def close(self):
        self._stack.close()



def make_consts(cfg):
    C = 128
    BS = 32
    hv = cfg["hv"]
    lo = np.tril(np.ones((C, C), np.float32), -1)
    loD = np.zeros((C, C), np.float32)
    for s in range(C // BS):
        sl = slice(s * BS, (s + 1) * BS)
        loD[sl, sl] = np.tril(np.ones((BS, BS), np.float32), -1)
    loL = lo - loD
    upD = loD.T
    upL = loL.T
    up_incl = np.triu(np.ones((C, C), np.float32), 0)

    def maskT(m):
        return np.where(m > 0, 0.0, NEG_BIG).astype(np.float32).T.copy()

    sel16 = np.zeros((hv, hv * C), np.float32)
    for p in range(hv):
        sel16[p, p * C:(p + 1) * C] = 1.0
    repid = np.tile(np.eye(C, dtype=np.float32), (1, hv))
    blockdiag_id = np.tile(np.eye(C, dtype=np.float32), (1, hv))

    return {
        "c_ident_f": np.eye(C, dtype=np.float32),
        "c_ident_b": np.eye(C, dtype=np.float32),
        "c_sel16": sel16,
        "c_repid": repid,
        "c_maskT_n": maskT(loD),
        "c_maskT_nt": maskT(upD),
        "c_maskT_LT": maskT(upL),
        "c_maskT_at": maskT(up_incl),
        "c_ones128": np.ones((C, 1), np.float32),
        "c_onesrow": np.ones((1, C), np.float32),
        "c_oneshv": np.ones((hv, C), np.float32),
        "c_repid_b": repid,
    }



MODE = "two_launch"


def kernel(**inputs):
    cfg = real_cfg()
    maps = shard_inputs(cfg, inputs)
    ncore = cfg["n_cores"]
    tf = cfg["tf"]
    if MODE == "fused":
        nc = bass.Bass("TRN2", target_bir_lowering=False)
        names = build_program(nc, cfg, "full")
        ins = [{k: m[k] for k in names} for m in maps]
        res = run_bass_kernel_spmd(nc, ins, core_ids=list(range(ncore)))
        return assemble_output(cfg, res)

    nc1 = bass.Bass("TRN2", target_bir_lowering=False)
    names1 = build_program(nc1, cfg, "attn")
    ins1 = [{k: m[k] for k in names1} for m in maps]
    res1 = run_bass_kernel_spmd(nc1, ins1, core_ids=list(range(ncore)))
    sparts = [np.asarray(res1.results[c]["spart_out"]) for c in range(ncore)]

    nc2 = bass.Bass("TRN2", target_bir_lowering=False)
    names2 = build_program(nc2, cfg, "ffn")
    ins2 = []
    for c in range(ncore):
        half = c % 2
        partner = c ^ 1
        m2 = {k: maps[c][k] for k in names2 if k not in ("s_half0", "s_half1")}
        m2["s_half0"] = np.ascontiguousarray(sparts[c][half * tf:(half + 1) * tf])
        m2["s_half1"] = np.ascontiguousarray(sparts[partner][half * tf:(half + 1) * tf])
        ins2.append(m2)
    res2 = run_bass_kernel_spmd(nc2, ins2, core_ids=list(range(ncore)))
    return assemble_output(cfg, res2)


def shard_inputs(cfg, inputs):
    d = cfg["d"]
    tb, tf = cfg["tb"], cfg["tf"]
    hv, hk, dk, dv = cfg["hv"], cfg["hk"], cfg["dk"], cfg["dv"]
    x = np.ascontiguousarray(np.asarray(inputs["L0_attn_hidden_states"], dtype=np.float32))
    W_qkv = np.asarray(inputs["L0_attn_W_qkv"], dtype=np.float32)
    W_z = np.asarray(inputs["L0_attn_W_z"], dtype=np.float32)
    W_b = np.asarray(inputs["L0_attn_W_b"], dtype=np.float32)
    W_a = np.asarray(inputs["L0_attn_W_a"], dtype=np.float32)
    conv_w = np.asarray(inputs["L0_attn_conv_w"], dtype=np.float32)
    dt_bias = np.asarray(inputs["L0_attn_dt_bias"], dtype=np.float32)
    A_log = np.asarray(inputs["L0_attn_A_log"], dtype=np.float32)
    norm_w = np.asarray(inputs["L0_attn_norm_w"], dtype=np.float32)
    W_out = np.asarray(inputs["L0_attn_W_out"], dtype=np.float32)
    W_router = np.asarray(inputs["L0_ffn_W_router"], dtype=np.float32)
    W_sgate = np.asarray(inputs["L0_ffn_W_sgate"], dtype=np.float32)
    Ws_gu = np.asarray(inputs["L0_ffn_Ws_gate_up"], dtype=np.float32)
    Ws_dn = np.asarray(inputs["L0_ffn_Ws_down"], dtype=np.float32)
    We_gu = np.asarray(inputs["L0_ffn_We_gate_up"], dtype=np.float32)
    We_dn = np.asarray(inputs["L0_ffn_We_down"], dtype=np.float32)

    consts = make_consts(cfg)
    maps = []
    for c in range(cfg["n_cores"]):
        b, half = c // 2, c % 2
        qcols = slice(half * hk * dk, (half + 1) * hk * dk)
        kcols = slice(2048 + half * hk * dk, 2048 + (half + 1) * hk * dk)
        vcols = slice(4096 + half * hv * dv, 4096 + (half + 1) * hv * dv)
        vheads = slice(half * hv, (half + 1) * hv)
        toks = slice(half * tf, (half + 1) * tf)
        wqkv_c = np.concatenate([W_qkv[:, qcols], W_qkv[:, kcols], W_qkv[:, vcols]], axis=1)
        convw_c = np.concatenate([conv_w[qcols.start:qcols.stop],
                                  conv_w[kcols.start:kcols.stop],
                                  conv_w[vcols.start:vcols.stop]], axis=0)
        m = {
            "x_nat": np.ascontiguousarray(x[b]),
            "xF": np.ascontiguousarray(x[b, toks]),
            "xT_half": np.ascontiguousarray(x[b, toks].T),
            "wqkv": np.ascontiguousarray(wqkv_c),
            "convw": np.ascontiguousarray(convw_c),
            "wz": np.ascontiguousarray(W_z[:, half * hv * dv:(half + 1) * hv * dv]),
            "wb": np.ascontiguousarray(W_b[:, vheads]),
            "wa": np.ascontiguousarray(W_a[:, vheads]),
            "alog": np.ascontiguousarray(A_log[vheads][:, None]),
            "dtb": np.ascontiguousarray(dt_bias[vheads][:, None]),
            "normw": np.ascontiguousarray(norm_w[:, None]),
            "wout": np.ascontiguousarray(W_out[half * hv * dv:(half + 1) * hv * dv]),
            "wrt": np.ascontiguousarray(W_router.T),
            "wsg": np.ascontiguousarray(W_sgate.T),
            "wsgu": Ws_gu,
            "wsdn": Ws_dn,
            "wegu": We_gu,
            "wedn": We_dn,
        }
        m.update(consts)
        maps.append(m)
    return maps


def assemble_output(cfg, res):
    tb, tf, d = cfg["tb"], cfg["tf"], cfg["d"]
    nb = cfg["n_cores"] // 2
    out = np.zeros((nb, tb, d), dtype=np.float32)
    for c in range(cfg["n_cores"]):
        b, half = c // 2, c % 2
        outT = res.results[c]["outT"]
        out[b, half * tf:(half + 1) * tf, :] = np.asarray(outT).T
    return out



def ceil_div(a, b):
    return (a + b - 1) // b


def col_strips(total, width=512):
    out = []
    c = 0
    while c < total:
        w = min(width, total - c)
        out.append((c, w))
        c += w
    return out


def build_program(nc, cfg, part="full"):
    d, hv, hk = cfg["d"], cfg["hv"], cfg["hk"]
    dk, dv = cfg["dk"], cfg["dv"]
    tb, tf = cfg["tb"], cfg["tf"]
    i_sh, i_e, n_e = cfg["i_sh"], cfg["i_e"], cfg["n_e"]
    DT = d // 128
    QT = 2 * hk + hv
    VT = hv
    NCH = tb // 128
    TH2 = tb // 2
    SH_T = i_sh // 128
    IE_T = i_e // 128
    HW = hv * 128
    C = 128

    p = Prog(nc)

    declared_inputs = []

    def din(name, shape, dt=F32):
        declared_inputs.append(name)
        return nc.dram_tensor(name, list(shape), dt, kind="ExternalInput")

    A = part in ("full", "attn")
    F = part in ("full", "ffn")
    t_x = din("x_nat", (tb, d)) if A else None
    t_xF = din("xF", (tf, d)) if F else None
    t_xT = din("xT_half", (d, tf)) if F else None
    t_wqkv = din("wqkv", (d, QT * 128)) if A else None
    t_convw = din("convw", (QT * 128, 4)) if A else None
    t_wz = din("wz", (d, hv * dv)) if A else None
    t_wb = din("wb", (d, hv)) if A else None
    t_wa = din("wa", (d, hv)) if A else None
    t_alog = din("alog", (hv, 1)) if A else None
    t_dtb = din("dtb", (hv, 1)) if A else None
    t_normw = din("normw", (128, 1)) if A else None
    t_wout = din("wout", (hv * dv, d)) if A else None
    t_wrt = din("wrt", (d, 8)) if F else None
    t_wsg = din("wsg", (d, 1)) if F else None
    t_wsgu = din("wsgu", (d, 2 * i_sh)) if F else None
    t_wsdn = din("wsdn", (i_sh, d)) if F else None
    t_wegu = din("wegu", (n_e, d, 2 * i_e)) if F else None
    t_wedn = din("wedn", (n_e, i_e, d)) if F else None
    t_cidf = din("c_ident_f", (C, C))
    t_cidb = din("c_ident_b", (C, C))
    t_sel = din("c_sel16", (hv, HW))
    t_repid = din("c_repid", (C, HW))
    t_mT_n = din("c_maskT_n", (C, C))
    t_mT_nt = din("c_maskT_nt", (C, C))
    t_mT_LT = din("c_maskT_LT", (C, C))
    t_mT_at = din("c_maskT_at", (C, C))
    t_ones = din("c_ones128", (C, 1))
    t_onesrow = din("c_onesrow", (1, C))
    t_oneshv = din("c_oneshv", (hv, C))
    t_repid_b = din("c_repid_b", (C, HW))

    if part == "ffn":
        t_s0 = din("s_half0", (tf, d), BF16)
        t_s1 = din("s_half1", (tf, d), BF16)
    t_spart_out = None
    t_out = None
    if part == "attn":
        t_spart_out = nc.dram_tensor("spart_out", [tb, d], BF16, kind="ExternalOutput")
    else:
        t_out = nc.dram_tensor("outT", [d, tf], F32, kind="ExternalOutput")

    s_xn = nc.dram_tensor("xn_scr", [tb, d], BF16)
    s_xnf = nc.dram_tensor("xnf_scr", [tf, d], BF16)
    s_qkv = nc.dram_tensor("qkv_scr", [QT * 128, tb], BF16)
    s_zsil = nc.dram_tensor("zsil_scr", [hv * dv, tb], BF16)
    s_o3 = nc.dram_tensor("o3_scr", [tb, hv * dv], BF16)
    s_spart = nc.dram_tensor("spart_scr", [tb, d], BF16) if part == "full" else None
    if part == "attn":
        s_spart = t_spart_out
    s_sred = nc.dram_tensor("sred_scr", [tf, d], BF16)
    s_hsh = nc.dram_tensor("hsh_scr", [i_sh, tf], BF16)
    s_he = nc.dram_tensor("he_scr", [n_e, i_e, tf], BF16)
    s_shsc = nc.dram_tensor("shsc_scr", [d, tf], BF16)
    s_crow = nc.dram_tensor("crow_scr", [9, tf], F32)
    s_gates = nc.dram_tensor("gates_scr", [3, hv, tb], F32)

    b_xn = p.buf("xn_scr")
    b_xnf = p.buf("xnf_scr")
    b_qkv = p.buf("qkv_scr")
    b_zsil = p.buf("zsil_scr")
    b_o3scr = p.buf("o3_scr")
    b_spart = p.buf("spart_scr")
    b_sred = p.buf("sred_scr")
    b_hsh = p.buf("hsh_scr")
    b_he = p.buf("he_scr")
    b_shsc = p.buf("shsc_scr")
    b_crow = p.buf("crow_scr")
    b_gates = p.buf("gates_scr")

    stack_global = ExitStack()

    def sb(name, shape, dt=F32):
        h = nc.alloc_sbuf_tensor(name, list(shape), dt)
        return h, p.buf(name)

    PSW = max(HW, 2048 if tb >= 4096 else max(TH2, tf, 512))
    PSW = max(PSW, 512)
    psA = nc.alloc_psum_tensor("psA", [128, PSW], F32)
    psB = nc.alloc_psum_tensor("psB", [128, PSW], F32)
    psA_r = [p.buf(f"psA{i}") for i in range(PSW // 512)]
    psB_r = [p.buf(f"psB{i}") for i in range(PSW // 512)]

    def regions(which, c0, w):
        rs = psA_r if which is psA else psB_r
        return [rs[i] for i in range(c0 // 512, ceil_div(c0 + w, 512))]

    h_const_id_f, b_cidf = sb("cidf", [C, C])
    h_const_id_b, b_cidb = sb("cidb", [C, C], BF16)
    h_sel, b_sel = sb("self32", [hv, HW])
    h_repid, b_repid = sb("repidf", [C, HW])
    h_repid_b, b_repidb = sb("repidb", [C, HW], BF16)
    h_masks, b_masks = sb("masks", [C, 4 * C])
    h_ones, b_ones = sb("ones", [C, 1])
    h_onesb, b_onesb = sb("onesb", [C, 1], BF16)
    h_onesr, b_onesr = sb("onesr", [1, C])
    h_oneshv, b_oneshv = sb("oneshv", [hv, C])
    h_convw, b_convw = sb("convw_sb", [128, QT * 4])
    h_alog, b_alog = sb("alog_sb", [hv, 1])
    h_dtb, b_dtb = sb("dtb_sb", [hv, 1])
    h_negea, b_negea = sb("negea", [hv, 1])
    h_normw, b_normw = sb("normw_sb", [128, 1])

    h_S, b_S = sb("S_state", [128, HW], BF16)
    h_crow1, b_crow1 = sb("crow1", [1, tf], F32)
    h_bias, b_bias = sb("biasconst", [128, 2], F32)

    p.dma("sync", lambda e: e.dma_start(out=h_const_id_f[:, :], in_=t_cidf.ap()[:, :]), writes=[b_cidf])
    p.dma("gpsimd", lambda e: e.dma_start(out=h_const_id_b[:, :], in_=t_cidb.ap()[:, :]), writes=[b_cidb])
    p.dma("sync", lambda e: e.dma_start(out=h_sel[:, :], in_=t_sel.ap()[:, :]), writes=[b_sel])
    p.dma("sync", lambda e: e.dma_start(out=h_repid[:, :], in_=t_repid.ap()[:, :]), writes=[b_repid])
    p.dma("gpsimd", lambda e: e.dma_start(out=h_repid_b[:, :], in_=t_repid_b.ap()[:, :]), writes=[b_repidb])
    for i, t in enumerate([t_mT_n, t_mT_nt, t_mT_LT, t_mT_at]):
        p.dma("sync", lambda e, i=i, t=t: e.dma_start(out=h_masks[:, i * C:(i + 1) * C], in_=t.ap()[:, :]), writes=[b_masks])
    p.dma("sync", lambda e: e.dma_start(out=h_ones[:, :], in_=t_ones.ap()[:, :]), writes=[b_ones])
    p.dma("gpsimd", lambda e: e.dma_start(out=h_onesb[:, :], in_=t_ones.ap()[:, :]), writes=[b_onesb])
    p.dma("sync", lambda e: e.dma_start(out=h_onesr[:, :], in_=t_onesrow.ap()[:, :]), writes=[b_onesr])
    p.dma("sync", lambda e: e.dma_start(out=h_oneshv[:, :], in_=t_oneshv.ap()[:, :]), writes=[b_oneshv])
    if A:
        p.dma("sync", lambda e: e.dma_start(
            out=h_convw[:, :].rearrange("p (t k) -> p t k", k=4),
            in_=t_convw.ap().rearrange("(t p) k -> p t k", p=128)), writes=[b_convw])
        p.dma("sync", lambda e: e.dma_start(out=h_alog[:, :], in_=t_alog.ap()[:, :]), writes=[b_alog])
        p.dma("sync", lambda e: e.dma_start(out=h_dtb[:, :], in_=t_dtb.ap()[:, :]), writes=[b_dtb])
        p.dma("sync", lambda e: e.dma_start(out=h_normw[:, :], in_=t_normw.ap()[:, :]), writes=[b_normw])
        p.step("scalar", lambda e: e.activation(h_negea[:, :], h_alog[:, :], AF.Exp, bias=0.0, scale=1.0),
               reads=[b_alog], writes=[b_negea])
        p.step("vector", lambda e: e.tensor_scalar_mul(h_negea[:, :], h_negea[:, :], -1.0),
               reads=[b_negea], writes=[b_negea])
        p.step("vector", lambda e: e.memset(h_S[:, :], 0.0), writes=[b_S])
    p.step("vector", lambda e: e.memset(h_bias[:, 0:1], 1.0), writes=[b_bias])
    p.step("vector", lambda e: e.memset(h_bias[:, 1:2], 1e-6), writes=[b_bias])

    with ExitStack() as ph:
        h_xc = [ph.enter_context(nc.sbuf_tensor(f"lnx{i}", [128, d], F32)) for i in range(2)]
        b_xc = [p.buf(f"lnx{i}") for i in range(2)]
        h_sq = ph.enter_context(nc.sbuf_tensor("lnsq", [128, d], F32))
        b_sq = p.buf("lnsq")
        h_st = ph.enter_context(nc.sbuf_tensor("lnst", [128, 8], F32))
        b_st = p.buf("lnst")
        h_xo = ph.enter_context(nc.sbuf_tensor("lnxo", [128, d], BF16))
        b_xo = p.buf("lnxo")

        def ln_pass(src, dst, b_dst, n_chunks):
            for i in range(n_chunks):
                bi = i % 2
                xap = src.ap()[i * 128:(i + 1) * 128, :]
                p.dma("sync", lambda e, xap=xap, bi=bi: e.dma_start(out=h_xc[bi][:, :], in_=xap),
                      writes=[b_xc[bi]])
                p.step("vector", lambda e, bi=bi: e.reduce_sum(h_st[:, 0:1], h_xc[bi][:, :], AX.X),
                       reads=[b_xc[bi]], writes=[b_st])
                p.step("scalar", lambda e, bi=bi: e.activation(h_sq[:, :], h_xc[bi][:, :], AF.Square, bias=0.0, scale=1.0),
                       reads=[b_xc[bi]], writes=[b_sq])
                p.step("vector", lambda e: e.reduce_sum(h_st[:, 1:2], h_sq[:, :], AX.X),
                       reads=[b_sq], writes=[b_st])
                p.step("vector", lambda e: e.tensor_scalar_mul(h_st[:, 2:3], h_st[:, 0:1], 1.0 / d),
                       reads=[b_st], writes=[b_st])
                p.step("vector", lambda e: e.tensor_mul(h_st[:, 3:4], h_st[:, 2:3], h_st[:, 2:3]),
                       reads=[b_st], writes=[b_st])
                p.step("vector", lambda e: e.scalar_tensor_tensor(h_st[:, 4:5], h_st[:, 1:2], 1.0 / d, h_st[:, 3:4], ALU.mult, ALU.subtract),
                       reads=[b_st], writes=[b_st])
                p.step("vector", lambda e: e.tensor_scalar_add(h_st[:, 4:5], h_st[:, 4:5], 1e-5),
                       reads=[b_st], writes=[b_st])
                p.step("vector", lambda e: e.reciprocal(h_st[:, 5:6], h_st[:, 4:5]),
                       reads=[b_st], writes=[b_st])
                p.step("scalar", lambda e: e.activation(h_st[:, 6:7], h_st[:, 5:6], AF.Sqrt, bias=0.0, scale=1.0),
                       reads=[b_st], writes=[b_st])
                p.step("vector", lambda e: e.scalar_tensor_tensor(h_st[:, 7:8], h_st[:, 2:3], -1.0, h_st[:, 6:7], ALU.mult, ALU.mult),
                       reads=[b_st], writes=[b_st])
                p.step("vector", lambda e, bi=bi: e.tensor_scalar(
                    h_xo[:, :], h_xc[bi][:, :], h_st[:, 6:7], h_st[:, 7:8], ALU.mult, ALU.add),
                    reads=[b_xc[bi], b_st], writes=[b_xo])
                p.dma("sync", lambda e, i=i, dst=dst: e.dma_start(out=dst.ap()[i * 128:(i + 1) * 128, :], in_=h_xo[:, :]),
                      reads=[b_xo], writes=[b_dst])

        if part in ("full", "attn"):
            ln_pass(t_x, s_xn, b_xn, tb // 128)
        if part in ("full", "ffn"):
            ln_pass(t_xF, s_xnf, b_xnf, tf // 128)
    p.barrier()

    if A:
        with ExitStack() as ph:
            h_xnT = ph.enter_context(nc.sbuf_tensor("xnTh", [128, DT * TH2], BF16))
            b_xnT = p.buf("xnTh")
            h_wsl = [ph.enter_context(nc.sbuf_tensor(f"wsl{i}", [128, DT * 128], BF16)) for i in range(2)]
            b_wsl = [p.buf(f"wsl{i}") for i in range(2)]
            h_mix = ph.enter_context(nc.sbuf_tensor("mix", [128, TH2 + 3], BF16))
            b_mix = p.buf("mix")
            h_cv = ph.enter_context(nc.sbuf_tensor("cvacc", [128, TH2], BF16))
            b_cv = p.buf("cvacc")
            h_row = ph.enter_context(nc.sbuf_tensor("rowout", [128, TH2], BF16))
            b_row = p.buf("rowout")
            h_hist = ph.enter_context(nc.sbuf_tensor("hist", [128, QT * 4], BF16))
            b_hist = p.buf("hist")
            h_wba = ph.enter_context(nc.sbuf_tensor("wba", [128, DT * 2 * hv], BF16))
            b_wba = p.buf("wba")
            h_sp = ph.enter_context(nc.sbuf_tensor("spbuf", [hv, 2 * TH2], F32))
            b_sp = p.buf("spbuf")
            h_gstg = ph.enter_context(nc.sbuf_tensor("gstg", [hv, TH2], F32))
            b_gstg = p.buf("gstg")

            p.step("vector", lambda e: e.memset(h_hist[:, :], 0.0), writes=[b_hist])
            p.dma("gpsimd", lambda e: e.dma_start(
                out=h_wba[:, 0:DT * hv].rearrange("p (k h) -> p k h", h=hv),
                in_=t_wb.ap().rearrange("(k p) h -> p k h", p=128)), writes=[b_wba])
            p.dma("gpsimd", lambda e: e.dma_start(
                out=h_wba[:, DT * hv:].rearrange("p (k h) -> p k h", h=hv),
                in_=t_wa.ap().rearrange("(k p) h -> p k h", p=128)), writes=[b_wba])

            for H in range(2):
                hsl = slice(H * TH2, (H + 1) * TH2)
                for K in range(DT):
                    p.dma("sync", lambda e, K=K, hsl=hsl: e.dma_start_transpose(
                        h_xnT[:, K * TH2:(K + 1) * TH2],
                        s_xn.ap()[hsl, K * 128:(K + 1) * 128]),
                        reads=[b_xn], writes=[b_xnT])
                for ot in range(QT + VT):
                    is_z = ot >= QT
                    zt = ot - QT
                    ps = psA if ot % 2 == 0 else psB
                    ps_r = regions(ps, 0, TH2)
                    wbuf = ot % 2
                    if not is_z:
                        wap = t_wqkv.ap()[:, ot * 128:(ot + 1) * 128]
                    else:
                        wap = t_wz.ap()[:, zt * 128:(zt + 1) * 128]
                    p.dma("gpsimd", lambda e, wap=wap, wbuf=wbuf: e.dma_start(
                        out=h_wsl[wbuf][:, :].rearrange("p (k c) -> p k c", c=128),
                        in_=wap.rearrange("(k p) c -> p k c", p=128)),
                        writes=[b_wsl[wbuf]])

                    def mmfn(e, ps=ps, wbuf=wbuf):
                        ins = None
                        for s0, w in col_strips(TH2):
                            for K in range(DT):
                                ins = e.matmul(ps[:, s0:s0 + w],
                                               h_wsl[wbuf][:, K * 128:(K + 1) * 128],
                                               h_xnT[:, K * TH2 + s0:K * TH2 + s0 + w],
                                               start=(K == 0), stop=(K == DT - 1))
                        return ins
                    p.step("tensor", mmfn, reads=[b_wsl[wbuf], b_xnT], writes=ps_r)

                    if is_z:
                        p.step("scalar", lambda e, ps=ps: e.activation(
                            h_row[:, :], ps[:, 0:TH2], AF.Silu, bias=0.0, scale=1.0),
                            reads=ps_r, writes=[b_row])
                        p.dma("sync", lambda e, zt=zt, hsl=hsl: e.dma_start(
                            out=s_zsil.ap()[zt * 128:(zt + 1) * 128, hsl], in_=h_row[:, :]),
                            reads=[b_row], writes=[b_zsil])
                    else:
                        p.step("vector", lambda e, ot=ot: e.tensor_copy(h_mix[:, 0:3], h_hist[:, ot * 4:ot * 4 + 3]),
                               reads=[b_hist], writes=[b_mix])
                        p.step("vector", lambda e, ps=ps: e.tensor_copy(h_mix[:, 3:3 + TH2], ps[:, 0:TH2]),
                               reads=ps_r, writes=[b_mix])
                        p.step("vector", lambda e, ot=ot: e.tensor_copy(h_hist[:, ot * 4:ot * 4 + 3], h_mix[:, TH2:TH2 + 3]),
                               reads=[b_mix], writes=[b_hist])
                        p.step("vector", lambda e, ot=ot: e.tensor_scalar_mul(
                            h_cv[:, :], h_mix[:, 0:TH2], h_convw[:, ot * 4:ot * 4 + 1]),
                            reads=[b_mix, b_convw], writes=[b_cv])
                        for tap in range(1, 4):
                            p.step("vector", lambda e, ot=ot, tap=tap: e.scalar_tensor_tensor(
                                h_cv[:, :], h_mix[:, tap:tap + TH2], h_convw[:, ot * 4 + tap:ot * 4 + tap + 1],
                                h_cv[:, :], ALU.mult, ALU.add),
                                reads=[b_mix, b_convw, b_cv], writes=[b_cv])
                        p.step("scalar", lambda e: e.activation(
                            h_row[:, :], h_cv[:, :], AF.Silu, bias=0.0, scale=1.0),
                            reads=[b_cv], writes=[b_row])
                        p.dma("sync", lambda e, ot=ot, hsl=hsl: e.dma_start(
                            out=s_qkv.ap()[ot * 128:(ot + 1) * 128, hsl], in_=h_row[:, :]),
                            reads=[b_row], writes=[b_qkv])

                for which in range(2):
                    ps = psA if which == 0 else psB
                    ps_r = regions(ps, 0, TH2)

                    def bafn(e, ps=ps, which=which):
                        ins = None
                        for s0, w in col_strips(TH2):
                            for K in range(DT):
                                ins = e.matmul(ps[0:hv, s0:s0 + w],
                                               h_wba[:, which * DT * hv + K * hv:which * DT * hv + (K + 1) * hv],
                                               h_xnT[:, K * TH2 + s0:K * TH2 + s0 + w],
                                               start=(K == 0), stop=(K == DT - 1))
                        return ins
                    p.step("tensor", bafn, reads=[b_wba, b_xnT], writes=ps_r)

                p.step("scalar", lambda e: e.activation(
                    h_gstg[:, :], psA[0:hv, 0:TH2], AF.Sigmoid, bias=0.0, scale=1.0),
                    reads=regions(psA, 0, TH2), writes=[b_gstg])
                p.dma("sync", lambda e, hsl=hsl: e.dma_start(
                    out=s_gates.ap()[2, :, hsl], in_=h_gstg[:, :]),
                    reads=[b_gstg], writes=[b_gates])
                p.step("scalar", lambda e: e.activation(
                    h_sp[:, 0:TH2], psA[0:hv, 0:TH2], AF.Abs, bias=0.0, scale=1.0),
                    reads=regions(psA, 0, TH2), writes=[b_sp])
                p.step("scalar", lambda e: e.activation(
                    h_sp[:, 0:TH2], h_sp[:, 0:TH2], AF.Exp, bias=0.0, scale=-1.0),
                    reads=[b_sp], writes=[b_sp])
                p.step("scalar", lambda e: e.activation(
                    h_sp[:, 0:TH2], h_sp[:, 0:TH2], AF.Ln, bias=h_bias[0:hv, 0:1], scale=1.0),
                    reads=[b_sp, b_bias], writes=[b_sp])
                p.step("vector", lambda e: e.tensor_scalar_min(
                    h_sp[:, TH2:2 * TH2], psA[0:hv, 0:TH2], 0.0),
                    reads=regions(psA, 0, TH2), writes=[b_sp])
                p.step("vector", lambda e: e.scalar_tensor_tensor(
                    h_gstg[:, :], h_sp[:, TH2:2 * TH2], 1.0, h_sp[:, 0:TH2], ALU.mult, ALU.subtract),
                    reads=[b_sp], writes=[b_gstg])
                p.dma("sync", lambda e, hsl=hsl: e.dma_start(
                    out=s_gates.ap()[1, :, hsl], in_=h_gstg[:, :]),
                    reads=[b_gstg], writes=[b_gates])
                p.step("scalar", lambda e: e.activation(
                    h_sp[:, 0:TH2], psB[0:hv, 0:TH2], AF.Abs, bias=h_dtb[:, 0:1], scale=1.0),
                    reads=regions(psB, 0, TH2) + [b_dtb], writes=[b_sp])
                p.step("scalar", lambda e: e.activation(
                    h_sp[:, 0:TH2], h_sp[:, 0:TH2], AF.Exp, bias=0.0, scale=-1.0),
                    reads=[b_sp], writes=[b_sp])
                p.step("scalar", lambda e: e.activation(
                    h_sp[:, 0:TH2], h_sp[:, 0:TH2], AF.Ln, bias=h_bias[0:hv, 0:1], scale=1.0),
                    reads=[b_sp, b_bias], writes=[b_sp])
                p.step("vector", lambda e: e.tensor_scalar(
                    h_sp[:, TH2:2 * TH2], psB[0:hv, 0:TH2], h_dtb[:, 0:1], 0.0, ALU.add, ALU.max),
                    reads=regions(psB, 0, TH2) + [b_dtb], writes=[b_sp])
                p.step("vector", lambda e: e.tensor_add(
                    h_sp[:, 0:TH2], h_sp[:, 0:TH2], h_sp[:, TH2:2 * TH2]),
                    reads=[b_sp], writes=[b_sp])
                p.step("vector", lambda e: e.tensor_scalar_mul(
                    h_gstg[:, :], h_sp[:, 0:TH2], h_negea[:, 0:1]),
                    reads=[b_sp, b_negea], writes=[b_gstg])
                p.dma("sync", lambda e, hsl=hsl: e.dma_start(
                    out=s_gates.ap()[0, :, hsl], in_=h_gstg[:, :]),
                    reads=[b_gstg], writes=[b_gates])
        p.barrier()

    if A:
        GW = min(512, tb)
        GC = GW // 128
        LN_HALF_DK = float(np.log(dk ** -0.5))

        with ExitStack() as ph:
            def psb(name, shape, dt=F32):
                h = ph.enter_context(nc.sbuf_tensor(name, list(shape), dt))
                return h, p.buf(name)

            h_qkg, b_qkg = psb("qkg", [128, 2 * hk * GW], BF16)
            h_vt, b_vt = psb("vtok", [128, HW], BF16)
            h_kt, b_kt = psb("ktok", [128, hk * 128], BF16)
            h_zt, b_zt = psb("ztok", [128, HW], BF16)
            h_KK, b_KK = psb("KKsb", [128, HW], BF16)
            h_KQ, b_KQ = psb("KQsb", [128, HW], BF16)
            h_ksq, b_ksq = psb("ksq", [128, 2 * hk * 128], BF16)
            h_n, b_n = psb("mat_n", [128, HW], BF16)
            h_nt, b_nt = psb("mat_nt", [128, HW], BF16)
            h_LT, b_LT = psb("mat_LT", [128, HW], BF16)
            h_at, b_at = psb("mat_at", [128, HW], BF16)
            h_Et, b_Et = psb("Etmp", [128, HW], BF16)
            h_pw = []
            b_pw = []
            for nm in ["nt2", "n2", "nt4", "n4", "nt8", "n8", "nt16", "n16"]:
                hh, bb = psb("pw_" + nm, [128, HW], BF16)
                h_pw.append(hh)
                b_pw.append(bb)
            PW = {nm: i for i, nm in enumerate(["nt2", "n2", "nt4", "n4", "nt8", "n8", "nt16", "n16"])}
            h_G0, b_G0 = psb("Gbuf0", [128, HW], BF16)
            h_G1, b_G1 = psb("Gbuf1", [128, HW], BF16)
            h_Gt, b_Gt = psb("Gtmp", [128, HW], BF16)
            h_Dinv, b_Dinv = psb("Dinv", [128, HW], BF16)
            h_R, b_R = psb("Rstrip", [128, HW], BF16)
            h_R2, b_R2 = psb("R2strip", [128, HW], BF16)
            h_U, b_U = psb("Ustrip", [128, HW], BF16)
            h_bV, b_bV = psb("bV", [128, HW], BF16)
            h_tq, b_tq = psb("tq", [128, HW], BF16)
            h_o, b_o = psb("ostrip", [128, HW], F32)
            h_osq, b_osq = psb("osq", [128, HW], F32)
            h_o3, b_o3 = psb("o3strip", [128, HW], BF16)
            h_Usc, b_Usc = psb("Usc", [128, HW], BF16)
            h_acd, b_acd = psb("acd", [128, HW], BF16)
            h_cmpA, b_cmpA = psb("cmpA", [96, 128], F32)
            h_cTa, b_cTa = psb("cTa", [128, 96], F32)
            h_cTk, b_cTk = psb("cTk", [hv, 128], F32)
            h_cTq, b_cTq = psb("cTq", [hv, 128], F32)
            h_hrow, b_hrow = psb("hrow", [hv, 3 * 128], F32)
            h_colbd, b_colbd = psb("colbd", [hv, HW], F32)
            h_cexp, b_cexp = psb("cexp", [128, 12 * hv], F32)
            h_hvhv, b_hvhv = psb("hvhv", [hv, hv], F32)
            h_gstg2, b_gstg2 = psb("gstg2", [hv, 256], F32)
            h_rstdT, b_rstdT = psb("rstdT", [hv, 128], F32)

            CE = {"lnks": 0, "lnqs": hv,
                  "mbA": 5 * hv, "cs": 6 * hv, "Aq": 7 * hv, "ACf": 8 * hv,
                  "rstd": 9 * hv, "tmp": 10 * hv, "tmp2": 11 * hv}

            def ce(nm, w=hv):
                c0 = CE[nm]
                return h_cexp[:, c0:c0 + w]

            def ce_col(nm, pp):
                c0 = CE[nm]
                return h_cexp[:, c0 + pp:c0 + pp + 1]

            qk3 = h_qkg[:, :].rearrange("p (t w) -> p t w", w=GW)

            def bT(w=hv):
                return h_cTa[:, 0:w]

            def lnbT(w=hv):
                return h_cTa[:, 32:32 + w]

            def betaT_col(pp):
                return h_cTa[:, 64 + pp:65 + pp]

            p.step("vector", lambda e: e.memset(h_cmpA[:, :], 0.0), writes=[b_cmpA])

            for n in range(NCH):
                gi = n % GC
                tok0 = gi * 128
                nsl = slice(n * 128, (n + 1) * 128)
                if gi == 0:
                    g0 = n * 128
                    p.dma("sync", lambda e, g0=g0: e.dma_start(
                        out=qk3,
                        in_=s_qkv.ap()[0:2 * hk * 128, g0:g0 + GW].rearrange("(t p) w -> p t w", p=128)),
                        reads=[b_qkv], writes=[b_qkg])

                p.dma("sync", lambda e, nsl=nsl: e.dma_start_transpose(
                    h_vt[:, :], s_qkv.ap()[2 * hk * 128:(2 * hk + hv) * 128, nsl]),
                    reads=[b_qkv], writes=[b_vt])
                p.dma("sync", lambda e, nsl=nsl: e.dma_start_transpose(
                    h_kt[:, :], s_qkv.ap()[hk * 128:2 * hk * 128, nsl]),
                    reads=[b_qkv], writes=[b_kt])
                p.dma("sync", lambda e, nsl=nsl: e.dma_start_transpose(
                    h_zt[:, :], s_zsil.ap()[:, nsl]),
                    reads=[b_zsil], writes=[b_zt])

                def qtile(h, tok0=tok0):
                    return qk3[:, h, tok0:tok0 + 128]

                def ktile(h, tok0=tok0):
                    return qk3[:, hk + h, tok0:tok0 + 128]

                p.step("vector", lambda e, tok0=tok0: e.tensor_mul(
                    h_ksq[:, :].rearrange("p (t c) -> p t c", c=128),
                    qk3[:, 0:2 * hk, tok0:tok0 + 128],
                    qk3[:, 0:2 * hk, tok0:tok0 + 128]),
                    reads=[b_qkg], writes=[b_ksq])

                def ssqfn(e):
                    ins = None
                    for h in range(2 * hk):
                        ins = e.matmul(psA[:, h:h + 1], h_ksq[:, h * 128:(h + 1) * 128],
                                       h_onesb[:, :], start=True, stop=True)
                    return ins
                p.step("tensor", ssqfn, reads=[b_ksq, b_onesb], writes=[psA_r[0]])
                p.step("scalar", lambda e: e.activation(
                    ce("tmp", 2 * hk), psA[:, 0:2 * hk], AF.Ln, bias=h_bias[:, 1:2], scale=1.0),
                    reads=[psA_r[0], b_bias], writes=[b_cexp])
                lnq3 = h_cexp[:, CE["lnqs"]:CE["lnqs"] + hv].rearrange("p (a b) -> p a b", b=2)
                lnk3 = h_cexp[:, CE["lnks"]:CE["lnks"] + hv].rearrange("p (a b) -> p a b", b=2)
                for r in range(2):
                    p.step("vector", lambda e, r=r, lnq3=lnq3: e.tensor_scalar(
                        lnq3[:, :, r:r + 1],
                        h_cexp[:, CE["tmp"]:CE["tmp"] + hk, None],
                        -0.5, LN_HALF_DK, ALU.mult, ALU.add),
                        reads=[b_cexp], writes=[b_cexp])
                    p.step("vector", lambda e, r=r, lnk3=lnk3: e.tensor_scalar(
                        lnk3[:, :, r:r + 1],
                        h_cexp[:, CE["tmp"] + hk:CE["tmp"] + 2 * hk, None],
                        -0.5, 0.0, ALU.mult, ALU.add),
                        reads=[b_cexp], writes=[b_cexp])

                p.dma("sync", lambda e, nsl=nsl: e.dma_start(
                    out=h_gstg2[:, 0:128], in_=s_gates.ap()[0, :, nsl]),
                    reads=[b_gates], writes=[b_gstg2])
                p.dma("sync", lambda e, nsl=nsl: e.dma_start(
                    out=h_cmpA[32:32 + hv, :], in_=s_gates.ap()[1, :, nsl]),
                    reads=[b_gates], writes=[b_cmpA])
                p.dma("sync", lambda e, nsl=nsl: e.dma_start(
                    out=h_gstg2[:, 128:256], in_=s_gates.ap()[1, :, nsl]),
                    reads=[b_gates], writes=[b_gstg2])
                p.dma("sync", lambda e, nsl=nsl: e.dma_start(
                    out=h_cmpA[64:64 + hv, :], in_=s_gates.ap()[2, :, nsl]),
                    reads=[b_gates], writes=[b_cmpA])
                p.step("vector", lambda e: e.tensor_tensor_scan(
                    h_cmpA[0:hv, :], h_gstg2[:, 0:128], h_gstg2[:, 0:128], 0.0, ALU.add, ALU.bypass),
                    reads=[b_gstg2], writes=[b_cmpA])
                p.step("tensor", lambda e: e.transpose(psB[:, 0:96], h_cmpA[:, :], h_const_id_f[0:96, 0:96]),
                       reads=[b_cmpA, b_cidf], writes=[psB_r[0]])
                p.step("vector", lambda e: e.tensor_copy(h_cTa[:, :], psB[:, 0:96]),
                       reads=[psB_r[0]], writes=[b_cTa])
                p.step("tensor", lambda e: e.transpose(psB[0:hv, 0:128], ce("lnks"), h_const_id_f[:, :]),
                       reads=[b_cexp, b_cidf], writes=[psB_r[0]])
                p.step("vector", lambda e: e.tensor_copy(h_cTk[:, :], psB[0:hv, 0:128]),
                       reads=[psB_r[0]], writes=[b_cTk])
                p.step("tensor", lambda e: e.transpose(psB[0:hv, 0:128], ce("lnqs"), h_const_id_f[:, :]),
                       reads=[b_cexp, b_cidf], writes=[psB_r[0]])
                p.step("vector", lambda e: e.tensor_copy(h_cTq[:, :], psB[0:hv, 0:128]),
                       reads=[psB_r[0]], writes=[b_cTq])

                p.step("vector", lambda e: e.tensor_add(h_hrow[:, 0:128], h_cmpA[0:hv, :], h_gstg2[:, 128:256]),
                       reads=[b_cmpA, b_gstg2], writes=[b_hrow])
                p.step("vector", lambda e: e.tensor_add(h_hrow[:, 0:128], h_hrow[:, 0:128], h_cTk[:, :]),
                       reads=[b_hrow, b_cTk], writes=[b_hrow])
                p.step("vector", lambda e: e.scalar_tensor_tensor(
                    h_hrow[:, 128:256], h_cmpA[0:hv, :], -1.0, h_cTk[:, :], ALU.mult, ALU.add),
                    reads=[b_cmpA, b_cTk], writes=[b_hrow])
                p.step("vector", lambda e: e.tensor_add(h_hrow[:, 256:384], h_cmpA[0:hv, :], h_cTq[:, :]),
                       reads=[b_cmpA, b_cTq], writes=[b_hrow])

                p.step("vector", lambda e: e.tensor_add(ce("tmp"), bT(), lnbT()),
                       reads=[b_cexp, b_cTa], writes=[b_cexp])
                p.step("vector", lambda e: e.tensor_add(ce("tmp"), ce("tmp"), ce("lnks")),
                       reads=[b_cexp], writes=[b_cexp])
                p.step("scalar", lambda e: e.activation(ce("tmp2"), ce("tmp"), AF.Exp, bias=0.0, scale=1.0),
                       reads=[b_cexp], writes=[b_cexp])
                p.step("vector", lambda e: e.tensor_scalar_mul(ce("mbA"), ce("tmp2"), -1.0),
                       reads=[b_cexp], writes=[b_cexp])
                p.step("vector", lambda e: e.tensor_scalar_mul(
                    h_hvhv[:, :], h_const_id_f[0:hv, 0:hv], h_cmpA[0:hv, 127:128]),
                    reads=[b_cidf, b_cmpA], writes=[b_hvhv])
                p.step("tensor", lambda e: e.matmul(psB[:, 0:hv], h_oneshv[:, :], h_hvhv[:, :], start=True, stop=True),
                       reads=[b_oneshv, b_hvhv], writes=[psB_r[0]])
                p.step("scalar", lambda e: e.activation(ce("ACf"), psB[:, 0:hv], AF.Copy, bias=0.0, scale=1.0),
                       reads=[psB_r[0]], writes=[b_cexp])
                p.step("vector", lambda e: e.scalar_tensor_tensor(
                    ce("tmp"), ce("ACf"), 1.0, bT(), ALU.mult, ALU.subtract),
                    reads=[b_cexp, b_cTa], writes=[b_cexp])
                p.step("vector", lambda e: e.tensor_add(ce("tmp"), ce("tmp"), ce("lnks")),
                       reads=[b_cexp], writes=[b_cexp])
                p.step("scalar", lambda e: e.activation(ce("cs"), ce("tmp"), AF.Exp, bias=0.0, scale=1.0),
                       reads=[b_cexp], writes=[b_cexp])
                p.step("scalar", lambda e: e.activation(ce("ACf"), ce("ACf"), AF.Exp, bias=0.0, scale=1.0),
                       reads=[b_cexp], writes=[b_cexp])
                p.step("vector", lambda e: e.tensor_add(ce("tmp"), bT(), ce("lnqs")),
                       reads=[b_cexp, b_cTa], writes=[b_cexp])
                p.step("scalar", lambda e: e.activation(ce("Aq"), ce("tmp"), AF.Exp, bias=0.0, scale=1.0),
                       reads=[b_cexp], writes=[b_cexp])


                def kkfn(e, ktile=ktile):
                    ins = None
                    for pp in range(hv):
                        ins = e.matmul(psA[:, pp * 128:(pp + 1) * 128], ktile(pp // 2), ktile(pp // 2),
                                       start=True, stop=True)
                    return ins
                p.step("tensor", kkfn, reads=[b_qkg], writes=regions(psA, 0, HW))
                p.step("vector", lambda e: e.tensor_copy(h_KK[:, :], psA[:, 0:HW]),
                       reads=regions(psA, 0, HW), writes=[b_KK])

                def kqfn(e, ktile=ktile, qtile=qtile):
                    ins = None
                    for pp in range(hv):
                        ins = e.matmul(psB[:, pp * 128:(pp + 1) * 128], ktile(pp // 2), qtile(pp // 2),
                                       start=True, stop=True)
                    return ins
                p.step("tensor", kqfn, reads=[b_qkg], writes=regions(psB, 0, HW))
                p.step("vector", lambda e: e.tensor_copy(h_KQ[:, :], psB[:, 0:HW]),
                       reads=regions(psB, 0, HW), writes=[b_KQ])

                variants = [
                    (0, 0, 0, h_n, b_n, h_KK, b_KK, -1.0),
                    (128, 1, 1, h_nt, b_nt, h_KK, b_KK, -1.0),
                    (128, 1, 2, h_LT, b_LT, h_KK, b_KK, 1.0),
                    (128, 2, 3, h_at, b_at, h_KQ, b_KQ, 1.0),
                ]
                for roff, cstrip, mi, h_tgt, b_tgt, h_kk, b_kk, sgn in variants:
                    coff = [128, 0, 256][cstrip]
                    p.step("vector", lambda e, coff=coff: e.tensor_mul(
                        h_colbd[:, :].rearrange("p (r c) -> p r c", c=128),
                        h_sel[:, :].rearrange("p (r c) -> p r c", c=128),
                        h_hrow[0:hv, None, coff:coff + 128].broadcast_to([hv, hv, 128])),
                        reads=[b_sel, b_hrow], writes=[b_colbd])

                    def dfn(e, roff=roff, mi=mi):
                        ins = None
                        for s0, w in col_strips(HW):
                            ins = e.matmul(psA[:, s0:s0 + w], h_hrow[:, roff:roff + 128],
                                           h_sel[:, s0:s0 + w], start=True, stop=False)
                            ins = e.matmul(psA[:, s0:s0 + w], h_oneshv[:, :],
                                           h_colbd[:, s0:s0 + w],
                                           start=False, stop=False)
                            ins = e.matmul(psA[:, s0:s0 + w], h_masks[:, mi * 128:(mi + 1) * 128],
                                           h_repid[:, s0:s0 + w], start=False, stop=True)
                        return ins
                    p.step("tensor", dfn, reads=[b_hrow, b_sel, b_oneshv, b_colbd, b_masks, b_repid],
                           writes=regions(psA, 0, HW))
                    p.step("scalar", lambda e: e.activation(h_Et[:, :], psA[:, 0:HW], AF.Exp, bias=0.0, scale=1.0),
                           reads=regions(psA, 0, HW), writes=[b_Et])
                    p.step("vector", lambda e, h_tgt=h_tgt, h_kk=h_kk, sgn=sgn: e.scalar_tensor_tensor(
                        h_tgt[:, :], h_Et[:, :], sgn, h_kk[:, :], ALU.mult, ALU.mult),
                        reads=[b_Et, b_kk], writes=[b_tgt])

                sq_plan = [
                    ("nt2", h_n, b_n, h_nt, b_nt),
                    ("n2", h_nt, b_nt, h_n, b_n),
                    ("nt4", h_pw[PW["n2"]], b_pw[PW["n2"]], h_pw[PW["nt2"]], b_pw[PW["nt2"]]),
                    ("n4", h_pw[PW["nt2"]], b_pw[PW["nt2"]], h_pw[PW["n2"]], b_pw[PW["n2"]]),
                    ("nt8", h_pw[PW["n4"]], b_pw[PW["n4"]], h_pw[PW["nt4"]], b_pw[PW["nt4"]]),
                    ("n8", h_pw[PW["nt4"]], b_pw[PW["nt4"]], h_pw[PW["n4"]], b_pw[PW["n4"]]),
                    ("nt16", h_pw[PW["n8"]], b_pw[PW["n8"]], h_pw[PW["nt8"]], b_pw[PW["nt8"]]),
                    ("n16", h_pw[PW["nt8"]], b_pw[PW["nt8"]], h_pw[PW["n8"]], b_pw[PW["n8"]]),
                ]
                for nm, hl, bl, hr, br in sq_plan:
                    def sqfn(e, hl=hl, hr=hr):
                        ins = None
                        for pp in range(hv):
                            sl2 = slice(pp * 128, (pp + 1) * 128)
                            ins = e.matmul(psB[:, sl2], hl[:, sl2], hr[:, sl2], start=True, stop=True)
                        return ins
                    p.step("tensor", sqfn, reads=[bl, br], writes=regions(psB, 0, HW))
                    p.step("vector", lambda e, nm=nm: e.tensor_copy(h_pw[PW[nm]][:, :], psB[:, 0:HW]),
                           reads=regions(psB, 0, HW), writes=[b_pw[PW[nm]]])

                def g1mm(e):
                    ins = None
                    for pp in range(hv):
                        sl2 = slice(pp * 128, (pp + 1) * 128)
                        ins = e.matmul(psA[:, sl2], h_pw[PW["n2"]][:, sl2], h_nt[:, sl2], start=True, stop=True)
                    return ins
                p.step("tensor", g1mm, reads=[b_pw[PW["n2"]], b_nt], writes=regions(psA, 0, HW))
                p.step("vector", lambda e: e.scalar_tensor_tensor(
                    h_Gt[:, :], psA[:, 0:HW], 1.0, h_pw[PW["nt2"]][:, :], ALU.mult, ALU.add),
                    reads=regions(psA, 0, HW) + [b_pw[PW["nt2"]]], writes=[b_Gt])
                p.step("vector", lambda e: e.tensor_add(h_G0[:, :], h_Gt[:, :], h_nt[:, :]),
                       reads=[b_Gt, b_nt], writes=[b_G0])
                gsrc, gsrcb = h_G0, b_G0
                gdst, gdstb = h_G1, b_G1
                for lv in ["4", "8", "16"]:
                    nlv, ntlv = PW["n" + lv], PW["nt" + lv]

                    def glmm(e, nlv=nlv, gsrc=gsrc):
                        ins = None
                        for pp in range(hv):
                            sl2 = slice(pp * 128, (pp + 1) * 128)
                            ins = e.matmul(psA[:, sl2], h_pw[nlv][:, sl2], gsrc[:, sl2], start=True, stop=True)
                        return ins
                    p.step("tensor", glmm, reads=[b_pw[nlv], gsrcb], writes=regions(psA, 0, HW))
                    p.step("vector", lambda e, ntlv=ntlv: e.scalar_tensor_tensor(
                        h_Gt[:, :], psA[:, 0:HW], 1.0, h_pw[ntlv][:, :], ALU.mult, ALU.add),
                        reads=regions(psA, 0, HW) + [b_pw[ntlv]], writes=[b_Gt])
                    p.step("vector", lambda e, gdst=gdst, gsrc=gsrc: e.tensor_add(
                        gdst[:, :], h_Gt[:, :], gsrc[:, :]),
                        reads=[b_Gt, gsrcb], writes=[gdstb])
                    gsrc, gsrcb, gdst, gdstb = gdst, gdstb, gsrc, gsrcb
                p.step("vector", lambda e, gsrc=gsrc: e.tensor_add(h_Dinv[:, :], gsrc[:, :], h_repid_b[:, :]),
                       reads=[gsrcb, b_repidb], writes=[b_Dinv])

                def ks0fn(e, ktile=ktile):
                    ins = None
                    for pp in range(hv):
                        ins = e.matmul(psA[:, pp * 128:(pp + 1) * 128], ktile(pp // 2),
                                       h_S[:, pp * 128:(pp + 1) * 128], start=True, stop=True)
                    return ins
                p.step("tensor", ks0fn, reads=[b_qkg, b_S], writes=regions(psA, 0, HW))
                for pp in range(hv):
                    p.step("vector", lambda e, pp=pp: e.tensor_scalar_mul(
                        h_bV[:, pp * 128:(pp + 1) * 128], h_vt[:, pp * 128:(pp + 1) * 128],
                        betaT_col(pp)),
                        reads=[b_vt, b_cTa], writes=[b_bV])
                for pp in range(hv):
                    p.step("vector", lambda e, pp=pp: e.scalar_tensor_tensor(
                        h_R[:, pp * 128:(pp + 1) * 128], psA[:, pp * 128:(pp + 1) * 128],
                        ce_col("mbA", pp), h_bV[:, pp * 128:(pp + 1) * 128], ALU.mult, ALU.add),
                        reads=regions(psA, pp * 128, 128) + [b_cexp, b_bV], writes=[b_R])

                def dinvfn(e, src, dst_ps):
                    ins = None
                    for pp in range(hv):
                        sl2 = slice(pp * 128, (pp + 1) * 128)
                        ins = e.matmul(dst_ps[:, sl2], h_Dinv[:, sl2], src[:, sl2], start=True, stop=True)
                    return ins
                p.step("tensor", lambda e: dinvfn(e, h_R, psB), reads=[b_Dinv, b_R], writes=regions(psB, 0, HW))
                p.step("vector", lambda e: e.tensor_copy(h_U[:, :], psB[:, 0:HW]),
                       reads=regions(psB, 0, HW), writes=[b_U])
                for it in range(3):
                    def ltfn(e):
                        ins = None
                        for pp in range(hv):
                            sl2 = slice(pp * 128, (pp + 1) * 128)
                            ins = e.matmul(psA[:, sl2], h_LT[:, sl2], h_U[:, sl2], start=True, stop=True)
                        return ins
                    p.step("tensor", ltfn, reads=[b_LT, b_U], writes=regions(psA, 0, HW))
                    p.step("vector", lambda e: e.scalar_tensor_tensor(
                        h_R2[:, :], psA[:, 0:HW], -1.0, h_R[:, :], ALU.mult, ALU.add),
                        reads=regions(psA, 0, HW) + [b_R], writes=[b_R2])
                    p.step("tensor", lambda e: dinvfn(e, h_R2, psB), reads=[b_Dinv, b_R2], writes=regions(psB, 0, HW))
                    p.step("vector", lambda e: e.tensor_copy(h_U[:, :], psB[:, 0:HW]),
                           reads=regions(psB, 0, HW), writes=[b_U])

                def qs0fn(e, qtile=qtile):
                    ins = None
                    for pp in range(hv):
                        ins = e.matmul(psA[:, pp * 128:(pp + 1) * 128], qtile(pp // 2),
                                       h_S[:, pp * 128:(pp + 1) * 128], start=True, stop=True)
                    return ins
                p.step("tensor", qs0fn, reads=[b_qkg, b_S], writes=regions(psA, 0, HW))

                def attufn(e):
                    ins = None
                    for pp in range(hv):
                        sl2 = slice(pp * 128, (pp + 1) * 128)
                        ins = e.matmul(psB[:, sl2], h_at[:, sl2], h_U[:, sl2], start=True, stop=True)
                    return ins
                p.step("tensor", attufn, reads=[b_at, b_U], writes=regions(psB, 0, HW))
                for pp in range(hv):
                    p.step("vector", lambda e, pp=pp: e.tensor_scalar_mul(
                        h_tq[:, pp * 128:(pp + 1) * 128], psA[:, pp * 128:(pp + 1) * 128],
                        ce_col("Aq", pp)),
                        reads=regions(psA, pp * 128, 128) + [b_cexp], writes=[b_tq])
                p.step("vector", lambda e: e.tensor_add(h_o[:, :], h_tq[:, :], psB[:, 0:HW]),
                       reads=[b_tq] + regions(psB, 0, HW), writes=[b_o])

                p.step("scalar", lambda e: e.activation(h_osq[:, :], h_o[:, :], AF.Square, bias=0.0, scale=1.0),
                       reads=[b_o], writes=[b_osq])
                p.step("vector", lambda e: e.tensor_reduce(
                    ce("tmp"), h_osq[:, :].rearrange("p (t c) -> p t c", c=128), AX.X, ALU.add),
                    reads=[b_osq], writes=[b_cexp])
                p.step("vector", lambda e: e.tensor_scalar(
                    ce("tmp"), ce("tmp"), 1.0 / dv, 1e-6, ALU.mult, ALU.add),
                    reads=[b_cexp], writes=[b_cexp])
                p.step("vector", lambda e: e.reciprocal(ce("tmp2"), ce("tmp")),
                       reads=[b_cexp], writes=[b_cexp])
                p.step("scalar", lambda e: e.activation(ce("rstd"), ce("tmp2"), AF.Sqrt, bias=0.0, scale=1.0),
                       reads=[b_cexp], writes=[b_cexp])
                p.step("tensor", lambda e: e.transpose(psB[0:hv, 0:128], ce("rstd"), h_const_id_f[:, :]),
                       reads=[b_cexp, b_cidf], writes=[psB_r[0]])
                p.step("vector", lambda e: e.tensor_copy(h_rstdT[:, :], psB[0:hv, 0:128]),
                       reads=[psB_r[0]], writes=[b_rstdT])

                def brfn(e):
                    ins = None
                    for s0, w in col_strips(HW):
                        ins = e.matmul(psA[:, s0:s0 + w], h_rstdT[:, :], h_sel[:, s0:s0 + w],
                                       start=True, stop=True)
                    return ins
                p.step("tensor", brfn, reads=[b_rstdT, b_sel], writes=regions(psA, 0, HW))
                p.step("vector", lambda e: e.tensor_mul(h_o3[:, :], h_o[:, :], psA[:, 0:HW]),
                       reads=[b_o] + regions(psA, 0, HW), writes=[b_o3])
                p.step("vector", lambda e: e.tensor_mul(h_o3[:, :], h_o3[:, :], h_zt[:, :]),
                       reads=[b_o3, b_zt], writes=[b_o3])
                p.dma("sync", lambda e, nsl=nsl: e.dma_start(out=s_o3.ap()[nsl, :], in_=h_o3[:, :]),
                      reads=[b_o3], writes=[b_o3scr])

                for pp in range(hv):
                    p.step("vector", lambda e, pp=pp: e.tensor_scalar_mul(
                        h_Usc[:, pp * 128:(pp + 1) * 128], h_U[:, pp * 128:(pp + 1) * 128],
                        ce_col("cs", pp)),
                        reads=[b_U, b_cexp], writes=[b_Usc])
                for pp in range(hv):
                    p.step("gpsimd", lambda e, pp=pp: e.tensor_scalar_mul(
                        h_acd[:, pp * 128:(pp + 1) * 128], h_const_id_b[:, :],
                        ce_col("ACf", pp)),
                        reads=[b_cidb, b_cexp], writes=[b_acd])

                def statefn(e):
                    ins = None
                    for pp in range(hv):
                        sl2 = slice(pp * 128, (pp + 1) * 128)
                        ins = e.matmul(psB[:, sl2], h_kt[:, (pp // 2) * 128:(pp // 2 + 1) * 128],
                                       h_Usc[:, sl2], start=True, stop=False)
                        ins = e.matmul(psB[:, sl2], h_acd[:, sl2], h_S[:, sl2],
                                       start=False, stop=True)
                    return ins
                p.step("tensor", statefn, reads=[b_kt, b_Usc, b_acd, b_S], writes=regions(psB, 0, HW))
                p.step("vector", lambda e: e.tensor_copy(h_S[:, :], psB[:, 0:HW]),
                       reads=regions(psB, 0, HW), writes=[b_S])
        p.barrier()

    if A:
        with ExitStack() as ph:
            h_wo = ph.enter_context(nc.sbuf_tensor("wo", [128, VT * d], BF16))
            b_wo = p.buf("wo")
            h_o3T = ph.enter_context(nc.sbuf_tensor("o3T", [128, VT * 512], BF16))
            b_o3T = p.buf("o3T")
            h_sev = ph.enter_context(nc.sbuf_tensor("sev", [128, 512], BF16))
            b_sev = p.buf("sev")

            for vt in range(VT):
                p.dma("gpsimd", lambda e, vt=vt: e.dma_start(
                    out=h_wo[:, vt * d:(vt + 1) * d],
                    in_=t_wout.ap()[vt * 128:(vt + 1) * 128, :]), writes=[b_wo])
                p.step("vector", lambda e, vt=vt: e.tensor_scalar_mul(
                    h_wo[:, vt * d:(vt + 1) * d], h_wo[:, vt * d:(vt + 1) * d], h_normw[:, 0:1]),
                    reads=[b_wo, b_normw], writes=[b_wo])

            for g in range(tb // 512):
                for vt in range(VT):
                    p.dma("sync", lambda e, g=g, vt=vt: e.dma_start_transpose(
                        h_o3T[:, vt * 512:(vt + 1) * 512],
                        s_o3.ap()[g * 512:(g + 1) * 512, vt * 128:(vt + 1) * 128]),
                        reads=[b_o3scr], writes=[b_o3T])
                for tt in range(4):
                    for oc0, ow in col_strips(d):
                        ps = psA if (tt + oc0 // 512) % 2 == 0 else psB

                        def smm(e, ps=ps, tt=tt, oc0=oc0, ow=ow):
                            ins = None
                            for vt in range(VT):
                                ins = e.matmul(ps[:, 0:ow],
                                               h_o3T[:, vt * 512 + tt * 128:vt * 512 + (tt + 1) * 128],
                                               h_wo[:, vt * d + oc0:vt * d + oc0 + ow],
                                               start=(vt == 0), stop=(vt == VT - 1))
                            return ins
                        p.step("tensor", smm, reads=[b_o3T, b_wo], writes=regions(ps, 0, ow))
                        p.step("vector", lambda e, ps=ps, ow=ow: e.tensor_copy(h_sev[:, 0:ow], ps[:, 0:ow]),
                               reads=regions(ps, 0, ow), writes=[b_sev])
                        p.dma("sync", lambda e, g=g, tt=tt, oc0=oc0, ow=ow: e.dma_start(
                            out=s_spart.ap()[g * 512 + tt * 128:g * 512 + (tt + 1) * 128, oc0:oc0 + ow],
                            in_=h_sev[:, 0:ow]),
                            reads=[b_sev], writes=[b_spart])

        if part == "full":
            p.step("gpsimd", lambda e: e.collective_compute(
                "ReduceScatter", ALU.add, replica_groups=cfg["pair_groups"],
                ins=[s_spart.ap().opt()], outs=[s_sred.ap().opt()]),
                reads=[b_spart], writes=[b_sred], semkey="cc", inc=16)
        p.barrier()

    if part == "ffn":
        with ExitStack() as ph:
            h_sa = ph.enter_context(nc.sbuf_tensor("ssum_a", [128, d], BF16))
            b_sa = p.buf("ssum_a")
            h_sb = ph.enter_context(nc.sbuf_tensor("ssum_b", [128, d], BF16))
            b_sb = p.buf("ssum_b")
            h_so = ph.enter_context(nc.sbuf_tensor("ssum_o", [128, d], BF16))
            b_so = p.buf("ssum_o")
            for i in range(tf // 128):
                isl = slice(i * 128, (i + 1) * 128)
                p.dma("sync", lambda e, isl=isl: e.dma_start(out=h_sa[:, :], in_=t_s0.ap()[isl, :]),
                      writes=[b_sa])
                p.dma("sync", lambda e, isl=isl: e.dma_start(out=h_sb[:, :], in_=t_s1.ap()[isl, :]),
                      writes=[b_sb])
                p.step("vector", lambda e: e.tensor_add(h_so[:, :], h_sa[:, :], h_sb[:, :]),
                       reads=[b_sa, b_sb], writes=[b_so])
                p.dma("sync", lambda e, isl=isl: e.dma_start(out=s_sred.ap()[isl, :], in_=h_so[:, :]),
                      reads=[b_so], writes=[b_sred])
        p.barrier()

    if F:
        TS_R = tf // 128
        with ExitStack() as ph:
            h_xfT = ph.enter_context(nc.sbuf_tensor("xfT", [128, DT * tf], BF16))
            b_xfT = p.buf("xfT")
            h_wrtb = ph.enter_context(nc.sbuf_tensor("wrtb", [128, DT * 8], BF16))
            b_wrtb = p.buf("wrtb")
            h_wsgb = ph.enter_context(nc.sbuf_tensor("wsgb", [128, DT], BF16))
            b_wsgb = p.buf("wsgb")
            h_comb = ph.enter_context(nc.sbuf_tensor("comb", [128, 32], F32))
            b_comb = p.buf("comb")
            h_cT = ph.enter_context(nc.sbuf_tensor("cT", [9, tf], F32))
            b_cT = p.buf("cT")
            h_wg = [ph.enter_context(nc.sbuf_tensor(f"wgu{i}", [128, DT * 128], BF16)) for i in range(2)]
            b_wg = [p.buf(f"wgu{i}") for i in range(2)]
            h_wu = [ph.enter_context(nc.sbuf_tensor(f"wup{i}", [128, DT * 128], BF16)) for i in range(2)]
            b_wu = [p.buf(f"wup{i}") for i in range(2)]
            h_gs = ph.enter_context(nc.sbuf_tensor("gsil", [128, tf], BF16))
            b_gs = p.buf("gsil")
            h_hb = ph.enter_context(nc.sbuf_tensor("hbuf", [128, tf], BF16))
            b_hb = p.buf("hbuf")
            h_Bc = ph.enter_context(nc.sbuf_tensor("Bc", [128, tf], BF16))
            b_Bc = p.buf("Bc")

            for K in range(DT):
                p.dma("sync", lambda e, K=K: e.dma_start_transpose(
                    h_xfT[:, K * tf:(K + 1) * tf],
                    s_xnf.ap()[:, K * 128:(K + 1) * 128]),
                    reads=[b_xnf], writes=[b_xfT])
            p.dma("gpsimd", lambda e: e.dma_start(
                out=h_wrtb[:, :].rearrange("p (k c) -> p k c", c=8),
                in_=t_wrt.ap().rearrange("(k p) c -> p k c", p=128)), writes=[b_wrtb])
            p.dma("gpsimd", lambda e: e.dma_start(
                out=h_wsgb[:, :].rearrange("p (k c) -> p k c", c=1),
                in_=t_wsg.ap().rearrange("(k p) c -> p k c", p=128),
                allow_slow_non_contiguous=True), writes=[b_wsgb])

            for ti in range(TS_R):
                def rtfn(e, ti=ti):
                    ins = None
                    for K in range(DT):
                        ins = e.matmul(psA[:, 0:8], h_xfT[:, K * tf + ti * 128:K * tf + (ti + 1) * 128],
                                       h_wrtb[:, K * 8:(K + 1) * 8], start=(K == 0), stop=(K == DT - 1))
                    for K in range(DT):
                        ins = e.matmul(psA[:, 8:9], h_xfT[:, K * tf + ti * 128:K * tf + (ti + 1) * 128],
                                       h_wsgb[:, K:K + 1], start=(K == 0), stop=(K == DT - 1))
                    return ins
                p.step("tensor", rtfn, reads=[b_xfT, b_wrtb, b_wsgb], writes=[psA_r[0]])
                p.step("vector", lambda e: e.reduce_max(h_comb[:, 16:17], psA[:, 0:8], AX.X),
                       reads=[psA_r[0]], writes=[b_comb])
                p.step("vector", lambda e: e.scalar_tensor_tensor(
                    h_comb[:, 0:8], psA[:, 0:8], 1.0, h_comb[:, 16:17].broadcast_to([128, 8]), ALU.mult, ALU.subtract),
                    reads=[psA_r[0], b_comb], writes=[b_comb])
                p.step("scalar", lambda e: e.activation(h_comb[:, 0:8], h_comb[:, 0:8], AF.Exp, bias=0.0, scale=1.0),
                       reads=[b_comb], writes=[b_comb])
                p.step("vector", lambda e: e.reduce_sum(h_comb[:, 17:18], h_comb[:, 0:8], AX.X),
                       reads=[b_comb], writes=[b_comb])
                p.step("vector", lambda e: e.reciprocal(h_comb[:, 18:19], h_comb[:, 17:18]),
                       reads=[b_comb], writes=[b_comb])
                p.step("vector", lambda e: e.tensor_scalar_mul(h_comb[:, 0:8], h_comb[:, 0:8], h_comb[:, 18:19]),
                       reads=[b_comb], writes=[b_comb])
                p.step("vector", lambda e: e.reduce_max(h_comb[:, 19:20], h_comb[:, 0:8], AX.X),
                       reads=[b_comb], writes=[b_comb])
                p.step("vector", lambda e: e.tensor_scalar(
                    h_comb[:, 8:16], h_comb[:, 0:8], h_comb[:, 19:20], None, ALU.is_ge),
                    reads=[b_comb], writes=[b_comb])
                p.step("vector", lambda e: e.scalar_tensor_tensor(
                    h_comb[:, 0:8], h_comb[:, 8:16], NEG_BIG, h_comb[:, 0:8], ALU.mult, ALU.add),
                    reads=[b_comb], writes=[b_comb])
                p.step("vector", lambda e: e.reduce_max(h_comb[:, 20:21], h_comb[:, 0:8], AX.X),
                       reads=[b_comb], writes=[b_comb])
                p.step("vector", lambda e: e.scalar_tensor_tensor(
                    h_comb[:, 0:8], h_comb[:, 8:16], -NEG_BIG, h_comb[:, 0:8], ALU.mult, ALU.add),
                    reads=[b_comb], writes=[b_comb])
                p.step("vector", lambda e: e.tensor_scalar(
                    h_comb[:, 8:16], h_comb[:, 0:8], h_comb[:, 20:21], None, ALU.is_ge),
                    reads=[b_comb], writes=[b_comb])
                p.step("vector", lambda e: e.tensor_mul(h_comb[:, 0:8], h_comb[:, 0:8], h_comb[:, 8:16]),
                       reads=[b_comb], writes=[b_comb])
                p.step("scalar", lambda e: e.activation(h_comb[:, 8:9], psA[:, 8:9], AF.Sigmoid, bias=0.0, scale=1.0),
                       reads=[psA_r[0]], writes=[b_comb])
                p.step("tensor", lambda e: e.transpose(psB[0:16, 0:128], h_comb[:, 0:16], h_const_id_f[:, :]),
                       reads=[b_comb, b_cidf], writes=[psB_r[0]])
                p.step("vector", lambda e, ti=ti: e.tensor_copy(h_cT[:, ti * 128:(ti + 1) * 128], psB[0:9, 0:128]),
                       reads=[psB_r[0]], writes=[b_cT])

            for r in range(9):
                p.dma("sync", lambda e, r=r: e.dma_start(
                    out=s_crow.ap()[r:r + 1, :], in_=h_cT[r:r + 1, :]),
                    reads=[b_cT], writes=[b_crow])

            def gu_tile(dst_scr, b_dst, gate_ap, up_ap, it, bc=None, b_bc=None):
                wb = it % 2
                p.dma("gpsimd", lambda e, gate_ap=gate_ap, wb=wb: e.dma_start(
                    out=h_wg[wb][:, :].rearrange("p (k c) -> p k c", c=128),
                    in_=gate_ap.rearrange("(k p) c -> p k c", p=128)), writes=[b_wg[wb]])
                p.dma("gpsimd", lambda e, up_ap=up_ap, wb=wb: e.dma_start(
                    out=h_wu[wb][:, :].rearrange("p (k c) -> p k c", c=128),
                    in_=up_ap.rearrange("(k p) c -> p k c", p=128)), writes=[b_wu[wb]])

                def gmm(e, wb=wb, which=0):
                    ins = None
                    hw_ = h_wg[wb] if which == 0 else h_wu[wb]
                    ps = psA if which == 0 else psB
                    for s0, w in col_strips(tf):
                        for K in range(DT):
                            ins = e.matmul(ps[:, s0:s0 + w], hw_[:, K * 128:(K + 1) * 128],
                                           h_xfT[:, K * tf + s0:K * tf + s0 + w],
                                           start=(K == 0), stop=(K == DT - 1))
                    return ins
                p.step("tensor", lambda e, wb=wb: gmm(e, wb, 0), reads=[b_wg[wb], b_xfT], writes=regions(psA, 0, tf))
                p.step("tensor", lambda e, wb=wb: gmm(e, wb, 1), reads=[b_wu[wb], b_xfT], writes=regions(psB, 0, tf))
                p.step("scalar", lambda e: e.activation(h_gs[:, :], psA[:, 0:tf], AF.Silu, bias=0.0, scale=1.0),
                       reads=regions(psA, 0, tf), writes=[b_gs])
                if bc is not None:
                    p.step("gpsimd", lambda e, bc=bc: e.tensor_mul(h_gs[:, :], h_gs[:, :], bc[:, :]),
                           reads=[b_gs, b_bc], writes=[b_gs])
                p.step("vector", lambda e: e.tensor_mul(h_hb[:, :], h_gs[:, :], psB[:, 0:tf]),
                       reads=[b_gs] + regions(psB, 0, tf), writes=[b_hb])
                p.dma("sync", lambda e, dst_scr=dst_scr, it=it: e.dma_start(
                    out=dst_scr[it * 128:(it + 1) * 128, :], in_=h_hb[:, :]),
                    reads=[b_hb], writes=[b_dst])

            for it in range(SH_T):
                gu_tile(s_hsh.ap(), b_hsh,
                        t_wsgu.ap()[:, it * 128:(it + 1) * 128],
                        t_wsgu.ap()[:, i_sh + it * 128:i_sh + (it + 1) * 128], it)

            for ei in range(n_e):
                p.dma("sync", lambda e, ei=ei: e.dma_start(
                    out=h_crow1[0:1, :], in_=s_crow.ap()[ei:ei + 1, :]),
                    reads=[b_crow], writes=[b_crow1])

                def bcfn(e, ei=ei):
                    ins = None
                    for s0, w in col_strips(tf):
                        ins = e.matmul(psA[:, s0:s0 + w], h_onesr[:, :],
                                       h_crow1[0:1, s0:s0 + w],
                                       start=True, stop=True)
                    return ins
                p.step("tensor", bcfn, reads=[b_onesr, b_crow1], writes=regions(psA, 0, tf))
                p.step("vector", lambda e: e.tensor_copy(h_Bc[:, :], psA[:, 0:tf]),
                       reads=regions(psA, 0, tf), writes=[b_Bc])
                for it in range(IE_T):
                    gu_tile(s_he.ap()[ei], b_he,
                            t_wegu.ap()[ei][:, it * 128:(it + 1) * 128],
                            t_wegu.ap()[ei][:, i_e + it * 128:i_e + (it + 1) * 128],
                            it, bc=h_Bc, b_bc=b_Bc)
        p.barrier()

    if F:
        TQ = tf // 2
        with ExitStack() as ph:
            h_hs = ph.enter_context(nc.sbuf_tensor("hsres", [128, SH_T * TQ], BF16))
            b_hs = p.buf("hsres")
            h_wds = ph.enter_context(nc.sbuf_tensor("wds", [128, SH_T * 128], BF16))
            b_wds = p.buf("wds")
            h_Bsg = ph.enter_context(nc.sbuf_tensor("Bsg", [128, TQ], BF16))
            b_Bsg = p.buf("Bsg")
            h_ss = ph.enter_context(nc.sbuf_tensor("ssout", [128, 512], BF16))
            b_ss = p.buf("ssout")

            for half in range(2):
                qsl = slice(half * TQ, (half + 1) * TQ)
                p.dma("sync", lambda e, qsl=qsl: e.dma_start(
                    out=h_hs[:, :].rearrange("p (k w) -> p k w", w=TQ),
                    in_=s_hsh.ap()[:, qsl].rearrange("(k p) w -> p k w", p=128)),
                    reads=[b_hsh], writes=[b_hs])

                p.dma("sync", lambda e, half=half: e.dma_start(
                    out=h_crow1[0:1, 0:TQ], in_=s_crow.ap()[8:9, half * TQ:(half + 1) * TQ]),
                    reads=[b_crow], writes=[b_crow1])

                def bsgfn(e, half=half):
                    ins = None
                    for s0, w in col_strips(TQ):
                        ins = e.matmul(psB[:, s0:s0 + w], h_onesr[:, :],
                                       h_crow1[0:1, s0:s0 + w],
                                       start=True, stop=True)
                    return ins
                p.step("tensor", bsgfn, reads=[b_onesr, b_crow1], writes=regions(psB, 0, TQ))
                p.step("vector", lambda e: e.tensor_copy(h_Bsg[:, :], psB[:, 0:TQ]),
                       reads=regions(psB, 0, TQ), writes=[b_Bsg])

                for ot in range(DT):
                    p.dma("gpsimd", lambda e, ot=ot: e.dma_start(
                        out=h_wds[:, :].rearrange("p (k c) -> p k c", c=128),
                        in_=t_wsdn.ap()[:, ot * 128:(ot + 1) * 128].rearrange("(k p) c -> p k c", p=128)),
                        writes=[b_wds])
                    for s0, w in col_strips(TQ):
                        def sdmm(e, s0=s0, w=w):
                            ins = None
                            for K in range(SH_T):
                                ins = e.matmul(psA[:, 0:w], h_wds[:, K * 128:(K + 1) * 128],
                                               h_hs[:, K * TQ + s0:K * TQ + s0 + w],
                                               start=(K == 0), stop=(K == SH_T - 1))
                            return ins
                        p.step("tensor", sdmm, reads=[b_wds, b_hs], writes=regions(psA, 0, w))
                        p.step("vector", lambda e, s0=s0, w=w: e.tensor_mul(
                            h_ss[:, 0:w], psA[:, 0:w], h_Bsg[:, s0:s0 + w]),
                            reads=regions(psA, 0, w) + [b_Bsg], writes=[b_ss])
                        p.dma("sync", lambda e, ot=ot, half=half, s0=s0, w=w: e.dma_start(
                            out=s_shsc.ap()[ot * 128:(ot + 1) * 128, half * TQ + s0:half * TQ + s0 + w],
                            in_=h_ss[:, 0:w]),
                            reads=[b_ss], writes=[b_shsc])
        p.barrier()

    if F:
        TW = 256 if tf >= 1024 else min(tf, 256)
        OG = 4
        with ExitStack() as ph:
            h_heq = ph.enter_context(nc.sbuf_tensor("heq", [128, n_e * IE_T * TW], BF16))
            b_heq = p.buf("heq")
            h_wd = ph.enter_context(nc.sbuf_tensor("wdq", [128, n_e * IE_T * OG * 128], BF16))
            b_wd = p.buf("wdq")
            h_shs = ph.enter_context(nc.sbuf_tensor("shs", [128, TW], BF16))
            b_shs = p.buf("shs")
            h_sv = ph.enter_context(nc.sbuf_tensor("sv", [128, TW], BF16))
            b_sv = p.buf("sv")
            h_xt2 = ph.enter_context(nc.sbuf_tensor("xt2", [128, TW], F32))
            b_xt2 = p.buf("xt2")
            h_t1 = ph.enter_context(nc.sbuf_tensor("t1f", [128, TW], F32))
            b_t1 = p.buf("t1f")
            h_t2 = ph.enter_context(nc.sbuf_tensor("t2b", [128, TW], BF16))
            b_t2 = p.buf("t2b")
            h_of = ph.enter_context(nc.sbuf_tensor("ofin", [128, TW], F32))
            b_of = p.buf("ofin")

            for ocg in range(ceil_div(DT, OG)):
                oc0g = ocg * OG * 128
                ow_g = min(OG * 128, d - oc0g)
                not_g = ow_g // 128
                for ei in range(n_e):
                    p.dma("gpsimd", lambda e, ei=ei, oc0g=oc0g, ow_g=ow_g: e.dma_start(
                        out=h_wd[:, ei * IE_T * OG * 128:ei * IE_T * OG * 128 + IE_T * ow_g].rearrange(
                            "p (k c) -> p k c", c=ow_g),
                        in_=t_wedn.ap()[ei][:, oc0g:oc0g + ow_g].rearrange("(k p) c -> p k c", p=128)),
                        writes=[b_wd])
                for ts in range(tf // TW):
                    tsl = slice(ts * TW, (ts + 1) * TW)
                    for ei in range(n_e):
                        p.dma("sync", lambda e, ei=ei, tsl=tsl: e.dma_start(
                            out=h_heq[:, ei * IE_T * TW:(ei + 1) * IE_T * TW].rearrange("p (k w) -> p k w", w=TW),
                            in_=s_he.ap()[ei][:, tsl].rearrange("(k p) w -> p k w", p=128)),
                            reads=[b_he], writes=[b_heq])
                    for ot4 in range(not_g):
                        ot = ocg * OG + ot4
                        osl = slice(ot * 128, (ot + 1) * 128)
                        ps = psA if ot4 % 2 == 0 else psB

                        def edmm(e, ps=ps, ot4=ot4, ow_g=ow_g):
                            ins = None
                            first = True
                            for ei in range(n_e):
                                for K in range(IE_T):
                                    ins = e.matmul(
                                        ps[:, 0:TW],
                                        h_wd[:, ei * IE_T * OG * 128 + K * ow_g + ot4 * 128:
                                             ei * IE_T * OG * 128 + K * ow_g + ot4 * 128 + 128],
                                        h_heq[:, (ei * IE_T + K) * TW:(ei * IE_T + K + 1) * TW],
                                        start=first, stop=(ei == n_e - 1 and K == IE_T - 1))
                                    first = False
                            return ins
                        p.step("tensor", edmm, reads=[b_wd, b_heq], writes=regions(ps, 0, TW))
                        p.dma("sync", lambda e, osl=osl, tsl=tsl: e.dma_start(
                            out=h_shs[:, :], in_=s_shsc.ap()[osl, tsl]),
                            reads=[b_shsc], writes=[b_shs])
                        p.dma("sync", lambda e, osl=osl, tsl=tsl: e.dma_start_transpose(
                            h_sv[:, :], s_sred.ap()[tsl, osl]),
                            reads=[b_sred], writes=[b_sv])
                        p.dma("sync", lambda e, osl=osl, tsl=tsl: e.dma_start(
                            out=h_xt2[:, :], in_=t_xT.ap()[osl, tsl]),
                            writes=[b_xt2])
                        p.step("vector", lambda e, ps=ps: e.tensor_add(h_t1[:, :], ps[:, 0:TW], h_shs[:, :]),
                               reads=regions(ps, 0, TW) + [b_shs], writes=[b_t1])
                        p.step("scalar", lambda e: e.activation(h_t2[:, :], h_t1[:, :], AF.Silu, bias=0.0, scale=1.0),
                               reads=[b_t1], writes=[b_t2])
                        p.step("vector", lambda e: e.tensor_mul(h_t2[:, :], h_t2[:, :], h_sv[:, :]),
                               reads=[b_t2, b_sv], writes=[b_t2])
                        p.step("vector", lambda e: e.tensor_add(h_of[:, :], h_t2[:, :], h_xt2[:, :]),
                               reads=[b_t2, b_xt2], writes=[b_of])
                        p.dma("sync", lambda e, osl=osl, tsl=tsl: e.dma_start(
                            out=t_out.ap()[osl, tsl], in_=h_of[:, :]),
                            reads=[b_of], writes=[])

    p.barrier()
    p.finalize()
    p.close()
    stack_global.close()
    return declared_inputs

